# version 02 (final)
# baseline (speedup 1.0000x reference)
import numpy as np
import ml_dtypes
from contextlib import ExitStack
import concourse.bass as bass
import concourse.mybir as mybir
from concourse.bass_utils import run_bass_kernel_spmd

F32 = mybir.dt.float32
BF16 = mybir.dt.bfloat16
I32 = mybir.dt.int32
AF = mybir.ActivationFunctionType
ALU = mybir.AluOpType
AX = mybir.AxisListType
BF16NP = ml_dtypes.bfloat16

NCORES = 8
TLOC = 2048
NTT = TLOC // 128
D = 1024
DCH = D // 128
SEQ = 4096
NB = 4

C_ZERO, C_ONE, C_EPS, C_INV1024, C_NEGEXPOFF, C_NEG1, C_EPSL2, C_SCL8 = range(8)
NCONST = 8
EXP_OFF = 16.0

XSTR = 2052


class _Call:
    __slots__ = ("name", "a", "k", "incs")

    def __init__(self, name, a, k):
        self.name, self.a, self.k, self.incs = name, a, k, []

    def then_inc(self, sem, n):
        self.incs.append((sem, n))
        return self

    def replay(self, eng):
        try:
            r = getattr(eng, self.name)(*self.a, **self.k)
        except Exception as e:
            raise RuntimeError(f"replay failed: {self.name} a={[type(x).__name__ for x in self.a]} k={ {kk: type(vv).__name__ for kk, vv in self.k.items()} }: {e}") from e
        for s, n in self.incs:
            r = r.then_inc(s, n)
        return r


class _Raw:
    __slots__ = ("fn",)

    def __init__(self, fn):
        self.fn = fn

    def replay(self, eng):
        self.fn(eng)


class EngRec:
    def __init__(self):
        self.calls = []

    def __getattr__(self, name):
        if name.startswith("_") or name in ("calls",):
            raise AttributeError(name)

        def rec(*a, **k):
            c = _Call(name, a, k)
            self.calls.append(c)
            return c
        return rec

    def raw(self, fn):
        self.calls.append(_Raw(fn))

    def replay(self, eng):
        for c in self.calls:
            c.replay(eng)


class Sem:
    def __init__(self, h):
        self.h = h
        self.n = 0


class Prog:

    def __init__(self, name="prog"):
        self.nc = bass.Bass("TRN2", target_bir_lowering=False)
        self.es = ExitStack()
        self.name = name
        self.T, self.V, self.S, self.G, self.Y = (EngRec() for _ in range(5))
        self._id = 0
        self.sems = {}

    def sem(self, name):
        if name not in self.sems:
            h = self.es.enter_context(self.nc.semaphore(f"sm_{name}"))
            self.sems[name] = Sem(h)
        return self.sems[name]

    def tick(self, rec, semname, k=1):
        sem = self.sem(semname)
        rec.calls[-1].then_inc(sem.h, k)
        sem.n += k
        return sem.n

    def vbar(self):
        sem = self.sem("vbar")
        self.V.engine_nop().then_inc(sem.h, 1)
        sem.n += 1
        self.V.wait_ge(sem.h, sem.n)

    def sbar(self):
        sem = self.sem("sbar")
        self.S.nop().then_inc(sem.h, 1)
        sem.n += 1
        self.S.wait_ge(sem.h, sem.n)

    def dma(self, rec, semname, *a, **k):
        rec.dma_start(*a, **k)
        return self.tick(rec, semname, 16)

    def wait(self, rec, semname, val=None):
        sem = self.sem(semname)
        rec.wait_ge(sem.h, sem.n if val is None else val)

    def sbuf(self, shape, dtype, name=None):
        self._id += 1
        return self.es.enter_context(
            self.nc.sbuf_tensor(name or f"sb{self._id}", list(shape), dtype))

    def psum(self, shape, dtype=F32, name=None):
        self._id += 1
        return self.es.enter_context(
            self.nc.psum_tensor(name or f"ps{self._id}", list(shape), dtype))

    def din(self, name, shape, dtype=F32):
        return self.nc.dram_tensor(name, list(shape), dtype, kind="ExternalInput").ap()

    def dout(self, name, shape, dtype=F32):
        return self.nc.dram_tensor(name, list(shape), dtype, kind="ExternalOutput").ap()

    def dshared(self, name, shape, dtype=F32):
        return self.nc.dram_tensor(name, list(shape), dtype, addr_space="Shared").ap()

    def dlocal(self, name, shape, dtype=F32):
        return self.nc.dram_tensor(name, list(shape), dtype).ap()

    def finalize(self):
        with self.nc.Block() as block:
            @block.sync
            def _(eng):
                self.Y.replay(eng)

            @block.gpsimd
            def _(eng):
                self.G.replay(eng)

            @block.tensor
            def _(eng):
                self.T.replay(eng)

            @block.vector
            def _(eng):
                self.V.replay(eng)

            @block.scalar
            def _(eng):
                self.S.replay(eng)
        self.es.close()
        return self.nc


def step0(ap, mid_count):
    a3 = ap.rearrange("p (o f) -> p o f", o=1)
    return bass.AP(a3.tensor, a3.offset, [list(a3.ap[0]), [0, mid_count], list(a3.ap[2])])


def xnt_col(c, t0=0):
    return c * XSTR + 1 + t0


def emit_consts(p, CONST):
    G = p.G
    G.memset(CONST[:, C_ZERO:C_ZERO + 1], 0.0)
    G.memset(CONST[:, C_ONE:C_ONE + 1], 1.0)
    G.memset(CONST[:, C_EPS:C_EPS + 1], 1e-5)
    G.memset(CONST[:, C_INV1024:C_INV1024 + 1], 1.0 / 1024.0)
    G.memset(CONST[:, C_NEGEXPOFF:C_NEGEXPOFF + 1], -EXP_OFF)
    G.memset(CONST[:, C_NEG1:C_NEG1 + 1], -1.0)
    G.memset(CONST[:, C_EPSL2:C_EPSL2 + 1], 1e-10)
    G.memset(CONST[:, C_SCL8:C_SCL8 + 1], 0.125)
    p.tick(G, "g")


def emit_ln(p, bufs, dep_v=None, dep_s=None):
    V, S, T_ = p.V, p.S, p.T
    XRES, XNT, CONST, IDENT = bufs["XRES"], bufs["XNT"], bufs["CONST"], bufs["IDENT"]
    STAGE, STATS, PTR = bufs["STAGE"], bufs["STATS"], bufs["PTR"]
    mean, m2s, e2s, rstd = (STATS[:, 0:16], STATS[:, 16:32], STATS[:, 32:48], STATS[:, 48:64])
    BNS = bufs["BNS"]
    sh = STATS[:, :].tensor

    for t in range(NTT):
        xt = XRES[:, t * 1024:(t + 1) * 1024]
        p.vbar()
        V.bn_stats(out=BNS[:, 0:6], in_=xt[:, 0:512])
        V.bn_stats(out=BNS[:, 6:12], in_=xt[:, 512:1024])
        p.vbar()
        bnsap = BNS[:, :]
        means_v = bass.AP(bnsap.tensor, bnsap.offset + 1, [list(bnsap.ap[0]), [3, 4]])
        m2_v = bass.AP(bnsap.tensor, bnsap.offset + 2, [list(bnsap.ap[0]), [3, 4]])
        V.tensor_reduce(out=STATS[:, t:t + 1], in_=means_v, axis=AX.X, op=ALU.add)
        V.tensor_reduce(out=STATS[:, 16 + t:17 + t], in_=m2_v, axis=AX.X, op=ALU.add)
        V.tensor_tensor(out=BNS[:, 12:16], in0=means_v, in1=means_v, op=ALU.mult)
        p.vbar()
        V.tensor_reduce(out=STATS[:, 32 + t:33 + t], in_=BNS[:, 12:16], axis=AX.X, op=ALU.add)
    p.vbar()
    tmpA, tmpB = BNS[:, 16:32], BNS[:, 32:48]
    V.tensor_scalar(out=mean, in0=mean, scalar1=0.25, scalar2=None, op0=ALU.mult)
    V.tensor_scalar(out=tmpA, in0=m2s, scalar1=CONST[:, C_INV1024:C_INV1024 + 1],
                    scalar2=None, op0=ALU.mult)
    V.tensor_scalar(out=tmpB, in0=e2s, scalar1=0.25, scalar2=None, op0=ALU.mult)
    p.vbar()
    V.tensor_tensor(out=e2s, in0=mean, in1=mean, op=ALU.mult)
    V.tensor_tensor(out=m2s, in0=tmpA, in1=tmpB, op=ALU.add)
    p.vbar()
    V.tensor_tensor(out=m2s, in0=m2s, in1=e2s, op=ALU.subtract)
    v_stats = p.tick(V, "v")
    S.wait_ge(p.sem("v").h, v_stats)
    S.activation(out=e2s, in_=m2s, func=AF.Sqrt, bias=CONST[:, C_EPS:C_EPS + 1])
    s_std = p.tick(S, "s")
    V.wait_ge(p.sem("s").h, s_std)
    V.reciprocal(out=rstd, in_=e2s)
    p.vbar()
    p.tick(V, "v")

    for t in range(NTT):
        xt = XRES[:, t * 1024:(t + 1) * 1024]
        V.wait_ge(p.sem("t").h, p.sem("t").n)
        V.tensor_scalar(out=STAGE[:, 0:1024], in0=xt, scalar1=mean[:, t:t + 1],
                        scalar2=rstd[:, t:t + 1], op0=ALU.subtract, op1=ALU.mult)
        v_app = p.tick(V, "v")
        T_.wait_ge(p.sem("v").h, v_app)
        for c in range(DCH):
            T_.transpose(PTR[:, c * 128:(c + 1) * 128], STAGE[:, c * 128:(c + 1) * 128], IDENT[:])
        t_tr = p.tick(T_, "t")
        V.wait_ge(p.sem("t").h, t_tr)
        for c in range(DCH):
            V.tensor_copy(out=XNT[:, xnt_col(c, t * 128):xnt_col(c, (t + 1) * 128)],
                          in_=PTR[:, c * 128:(c + 1) * 128])
        p.tick(V, "v")
    return p.sem("v").n


def emit_stream_w(p, dram_ap, sb_bf16, nbytes_desc=""):
    p.G.dma_start(out=sb_bf16, in_=dram_ap)
    return p.tick(p.G, "gd", 16)


def emit_residual_add(p, XRES, psum_ap, tile_idx, extra_sb=None):
    xt = XRES[:, tile_idx * 1024:(tile_idx + 1) * 1024]
    if extra_sb is not None:
        p.V.tensor_tensor(out=xt, in0=xt, in1=extra_sb, op=ALU.add)
    p.V.tensor_tensor(out=xt, in0=xt, in1=psum_ap, op=ALU.add)
    return p.tick(p.V, "v")


def emit_ffn(p, bufs, phase=5.0):
    T_, V, S, G, Y = p.T, p.V, p.S, p.G, p.Y
    XRES, XNT, ONES, PTR = bufs["XRES"], bufs["XNT"], bufs["ONES"], bufs["PTR"]
    IDENT = bufs["IDENT"]
    LTBF, EV1B, LTT = bufs["LTBF"], bufs["EV1B"], bufs["LTT"]
    IND4 = bufs["IND4"]
    B1R, B2R, SCL = bufs["B1R"], bufs["B2R"], bufs["SCL"]
    HT, W2SB, WA, WB = bufs["HT"], bufs["W2SB"], bufs["WA"], bufs["WB"]
    T1S, T2S, E2, O2B = bufs["T1S"], bufs["T2S"], bufs["E2"], bufs["O2B"]
    PSA, PSB = bufs["PSA"], bufs["PSB"]
    lt_dram, w1t_dram, w2t_dram = bufs["lt_dram"], bufs["w1t_dram"], bufs["w2t_dram"]
    b1_dram, b2_dram = bufs["b1_dram"], bufs["b2_dram"]
    GRP = bufs["GRP"]
    NG = TLOC // GRP
    TPG = GRP // 128

    p.dma(G, "gd", out=LTBF, in_=lt_dram)
    p.dma(G, "gd", out=B1R, in_=b1_dram)
    g_lt = p.dma(G, "gd", out=B2R, in_=b2_dram)
    w2r = w2t_dram.rearrange("(hc p) o -> p hc o", p=128)
    for q in range(4):
        g_w2 = p.dma(G, "gd", out=W2SB[:].rearrange("p (hc o) -> p hc o", o=1024)[:, q * 8:(q + 1) * 8, :],
                     in_=w2r[:, q * 8:(q + 1) * 8, :])
    G.memset(E2[:], 0.0)
    for i in range(4):
        for r_lo in range(4):
            sap = bass.AP(lt_dram.tensor, (8 + i) * 1024 + r_lo, [[1, 1], [4, 256]])
            g_e2 = p.dma(G, "gd", out=E2[4 * i + r_lo:4 * i + r_lo + 1, i * 256:(i + 1) * 256],
                         in_=sap, allow_slow_non_contiguous=True)

    if phase < 2.2:
        return p.sem("v").n
    T_.wait_ge(p.sem("gd").h, g_lt)
    p.wait(T_, "v")
    for c in range(DCH):
        T_.transpose(PTR[:, c * 128:c * 128 + 16], LTBF[:, c * 128:(c + 1) * 128], IDENT[0:16, 0:16])
    t_ltt = p.tick(T_, "t")
    V.wait_ge(p.sem("t").h, t_ltt)
    for c in range(DCH):
        V.tensor_copy(out=LTT[:, c * 16:(c + 1) * 16], in_=PTR[:, c * 128:c * 128 + 16])
    p.vbar()
    v_ltt = p.tick(V, "v")
    if phase < 2.3:
        return p.sem("v").n
    V.tensor_copy(out=IND4[0:4, 0:4], in_=bufs["IDENT"][0:4, 0:4])
    p.vbar()
    for hc in range(32):
        V.tensor_scalar(out=EV1B[0:4, hc * 128:(hc + 1) * 128],
                        in0=LTBF[0:4, (hc % 8) * 128:(hc % 8 + 1) * 128],
                        scalar1=IND4[0:4, hc // 8:hc // 8 + 1], scalar2=None, op0=ALU.mult)
    for hc in range(32):
        V.memset(O2B[:, hc * 16:(hc + 1) * 16], 0.0)
        for i in range(4):
            V.tensor_copy(out=O2B[:, hc * 16 + 4 * i + hc // 8:hc * 16 + 4 * i + hc // 8 + 1],
                          in_=LTT[:, (hc % 8) * 16 + 12 + i:(hc % 8) * 16 + 12 + i + 1])
    v_o2b = p.tick(V, "v")

    if phase < 2.4:
        return p.sem("v").n
    p.wait(T_, "v")
    for c in range(DCH):
        for q in range(4):
            T_.matmul(PSA[0:4, q * 512:(q + 1) * 512],
                      LTT[:, c * 16 + 4:c * 16 + 8],
                      XNT[:, xnt_col(c, q * 512):xnt_col(c, (q + 1) * 512)],
                      start=(c == 0), stop=(c == DCH - 1))
    t_t1 = p.tick(T_, "t")
    V.wait_ge(p.sem("t").h, t_t1)
    V.tensor_scalar(out=T1S[0:4, :], in0=PSA[0:4, :], scalar1=SCL[0:4, :], scalar2=None, op0=ALU.mult)
    v_t1 = p.tick(V, "v")

    T_.wait_ge(p.sem("v").h, v_t1)
    if phase < 3:
        return p.sem("v").n
    sgd = p.sem("gd")
    st = p.sem("t")
    ss = p.sem("s")
    sv = p.sem("v")
    w1r = w1t_dram.rearrange("(c p) h -> p c h", p=128)

    def load_w1(hc, buf):
        return p.dma(G, "gd", out=buf[:].rearrange("p (c h) -> p c h", h=128),
                     in_=w1r[:, :, hc * 128:(hc + 1) * 128])

    w1_ready = [load_w1(0, WA)]
    t_after_gelu = {}
    for g in range(NG):
        gc0 = g * GRP
        T_.wait_ge(sv.h, t_after_gelu.get(("o", 0), 0))
        T_.wait_ge(sv.h, t_after_gelu.get(("o", 1), 0))
        for hc in range(32):
            buf = WA if hc % 2 == 0 else WB
            nxt = WB if hc % 2 == 0 else WA
            if hc + 1 < 32 or g + 1 < NG:
                nhc = (hc + 1) % 32
                G.wait_ge(st.h, t_after_gelu.get(("w", nxt is WB), 0))
                w1_ready.append(load_w1(nhc, nxt))
            T_.wait_ge(sgd.h, w1_ready[-2 if (hc + 1 < 32 or g + 1 < NG) else -1])
            ps = PSA[:, 0:512] if hc % 2 == 0 else PSA[:, 512:1024]
            T_.wait_ge(ss.h, t_after_gelu.get(("s", hc % 2), 0))
            for c in range(DCH):
                T_.matmul(ps[:, 0:GRP], buf[:, c * 128:(c + 1) * 128],
                          XNT[:, xnt_col(c, gc0):xnt_col(c, gc0 + GRP)],
                          start=(c == 0), stop=(c == DCH - 1 and phase < 3.3))
            if phase >= 3.5:
                T_.matmul(ps[:, 0:GRP], EV1B[0:4, hc * 128:(hc + 1) * 128],
                          T1S[0:4, gc0:gc0 + GRP], start=False, stop=False)
            if phase >= 3.3:
                T_.matmul(ps[:, 0:GRP], B1R[0:1, hc * 128:hc * 128 + 128],
                          ONES[0:1, 0:GRP], start=False, stop=True)
            else:
                T_.matmul(ps[:, 0:GRP], B1R[0:1, hc * 128:hc * 128 + 128],
                          ONES[0:1, 0:GRP], start=False, stop=True) if False else None
            if phase < 3.3:
                pass
            t_mm = p.tick(T_, "t")
            t_after_gelu[("w", buf is WB)] = t_mm
            S.wait_ge(st.h, t_mm)
            S.activation(out=HT[:, hc * GRP:(hc + 1) * GRP], in_=ps[:, 0:GRP], func=AF.Gelu)
            t_after_gelu[("s", hc % 2)] = p.tick(S, "s")
        s_gelu_all = p.sem("s").n
        if phase < 4:
            continue
        T_.wait_ge(ss.h, s_gelu_all)
        for hc in range(32):
            T_.matmul(PSB[0:16, 0:GRP], O2B[:, hc * 16:(hc + 1) * 16],
                      HT[:, hc * GRP:(hc + 1) * GRP], start=(hc == 0), stop=(hc == 31))
        t_t2 = p.tick(T_, "t")
        V.wait_ge(st.h, t_t2)
        V.tensor_scalar(out=T2S[0:16, gc0:gc0 + GRP], in0=PSB[0:16, 0:GRP],
                        scalar1=SCL[0:16, :], scalar2=None, op0=ALU.mult)
        v_t2 = p.tick(V, "v")
        if phase < 4.5:
            continue
        T_.wait_ge(sv.h, v_t2)
        if g == 0:
            T_.wait_ge(sgd.h, g_w2)
            T_.wait_ge(sgd.h, g_e2)
        for tt in range(TPG):
            tok0 = gc0 + tt * 128
            tile = tok0 // 128
            ps = PSA[:, 0:1024] if tt % 2 == 0 else PSA[:, 1024:2048]
            T_.wait_ge(sv.h, t_after_gelu.get(("o", tt % 2), 0))
            for hc in range(32):
                for o2 in range(2):
                    T_.matmul(ps[:, o2 * 512:(o2 + 1) * 512],
                              HT[:, hc * GRP + tt * 128:hc * GRP + (tt + 1) * 128],
                              W2SB[:, hc * 1024 + o2 * 512:hc * 1024 + (o2 + 1) * 512],
                              start=(hc == 0), stop=(hc == 31 and phase < 4.7))
            for o2 in range(2):
                if phase >= 4.9:
                    T_.matmul(ps[:, o2 * 512:(o2 + 1) * 512],
                              T2S[0:16, tok0:tok0 + 128],
                              E2[0:16, o2 * 512:(o2 + 1) * 512], start=False, stop=False)
                if phase >= 4.7:
                    T_.matmul(ps[:, o2 * 512:(o2 + 1) * 512],
                              ONES[0:1, 0:128], B2R[0:1, o2 * 512:(o2 + 1) * 512],
                              start=False, stop=(o2 == 1))
            t_out = p.tick(T_, "t")
            V.wait_ge(st.h, t_out)
            V.tensor_tensor(out=XRES[:, tile * 1024:(tile + 1) * 1024],
                            in0=XRES[:, tile * 1024:(tile + 1) * 1024],
                            in1=ps[:, 0:1024], op=ALU.add)
            t_after_gelu[("o", tt % 2)] = p.tick(V, "v")
    return p.sem("v").n


def alloc_common(p):
    bufs = {}
    bufs["XRES"] = p.sbuf([128, NTT * 1024], F32, "XRES")
    bufs["XNT"] = p.sbuf([128, DCH * XSTR], BF16, "XNT")
    bufs["CONST"] = p.sbuf([128, NCONST], F32, "CONST")
    bufs["STATS"] = p.sbuf([128, 64], F32, "STATS")
    bufs["BNS"] = p.sbuf([128, 48], F32, "BNS")
    bufs["PTR"] = p.psum([128, 1024], BF16, "PTR")
    return bufs


def alloc_misc_rows(p, bufs):
    misc = p.sbuf([128, 7168], BF16, "MISC")
    bufs["MISC"] = misc
    bufs["ONES"] = misc[0:1, 0:2048]
    bufs["B1R"] = misc[0:1, 2048:6144]
    bufs["B2R"] = misc[0:1, 6144:7168]


def emit_init(p, bufs, x_dram, ident_dram):
    CONST = bufs["CONST"]
    emit_consts(p, CONST)
    if "ONES" in bufs:
        p.G.memset(bufs["ONES"], 1.0)
    else:
        p.G.engine_nop()
    p.tick(p.G, "g")
    p.nc.const_aps.register(F32, 0.0, CONST[:, C_ZERO:C_ZERO + 1])
    p.nc.const_aps.register(F32, 1.0, CONST[:, C_ONE:C_ONE + 1])
    XRES = bufs["XRES"]
    p.dma(p.Y, "yd", out=XRES[:].rearrange("p (t d) -> p t d", d=1024),
          in_=x_dram.rearrange("(t p) d -> p t d", p=128))
    bufs["IDENT"] = ident_sb = p.sbuf([128, 128], BF16, "IDENT")
    y_x = p.dma(p.Y, "yd", out=ident_sb[:], in_=ident_dram)
    for rec in (p.T, p.V, p.S):
        rec.wait_ge(p.sem("g").h, p.sem("g").n)
        rec.wait_ge(p.sem("yd").h, y_x)
    return y_x


def build_ffn_launch(phase=5, debug=False):
    p = Prog("ffn")
    x = p.din("x", [TLOC, D])
    lt = p.din("lt", [16, D])
    w1t = p.din("w1t", [D, 4096])
    w2t = p.din("w2t", [4096, D])
    b1 = p.din("b1", [1, 4096])
    b2 = p.din("b2", [1, D])
    scl = p.din("scl", [128, 1])
    ident = p.din("ident", [128, 128], BF16)
    xo = p.dout("xo", [TLOC, D])

    bufs = alloc_common(p)
    alloc_misc_rows(p, bufs)
    GRP = 128
    bufs.update(
        GRP=GRP,
        lt_dram=lt, w1t_dram=w1t, w2t_dram=w2t, b1_dram=b1, b2_dram=b2,
        LTT=p.sbuf([128, DCH * 16], BF16, "LTT"),
        SCL=p.sbuf([128, 1], F32, "SCL"),
        HT=p.sbuf([128, 32 * GRP], BF16, "HT"),
        W2SB=p.sbuf([128, 32 * 1024], BF16, "W2SB"),
        WA=p.sbuf([128, 1024], BF16, "WA"),
        WB=p.sbuf([128, 1024], BF16, "WB"),
        T2S=p.sbuf([16, 2048], BF16, "T2S"),
        E2=p.sbuf([16, 1024], BF16, "E2"),
        O2B=p.sbuf([128, 32 * 16], BF16, "O2B"),
        PSA=p.psum([128, 2048], F32, "PSA"),
        PSB=p.psum([128, 1024], F32, "PSB"),
    )
    bufs["T1S"] = p.sbuf([4, 2048], BF16, "T1S")
    bufs["EV1B"] = p.sbuf([4, 4096], BF16, "EV1B")
    bufs["IND4"] = p.sbuf([4, 8], F32, "IND4")
    bufs["STAGE"] = bufs["HT"][:, 3072:4096]
    bufs["LTBF"] = bufs["HT"][0:16, 2048:3072]
    emit_init(p, bufs, x, ident)
    y_scl = p.dma(p.Y, "yd", out=bufs["SCL"][:], in_=scl)
    p.V.wait_ge(p.sem("yd").h, y_scl)
    emit_ln(p, bufs)
    if phase >= 2:
        emit_ffn(p, bufs, phase=phase)
    if debug:
        d_ht = p.dout("d_ht", [128, 32 * GRP])
        d_w2 = p.dout("d_w2", [128, 1024])
        d_t1 = p.dout("d_t1", [4, 2048])
        d_t2 = p.dout("d_t2", [16, 2048])
        d_e2 = p.dout("d_e2", [16, 1024])
        d_xnt = p.dout("d_xnt", [128, DCH * XSTR])
        d_st = p.dout("d_st", [128, 64])
        d_bns = p.dout("d_bns", [128, 16])
        for sem in ("v", "s", "t", "gd"):
            p.wait(p.G, sem)
        p.dma(p.G, "yo", out=d_ht, in_=bufs["HT"][:, :])
        p.dma(p.G, "yo", out=d_w2, in_=bufs["W2SB"][:, 0:1024])
        p.dma(p.G, "yo", out=d_t1, in_=bufs["T1S"][0:4, :])
        p.dma(p.G, "yo", out=d_t2, in_=bufs["T2S"][:, :])
        p.dma(p.G, "yo", out=d_e2, in_=bufs["E2"][:, :])
        p.dma(p.G, "yo", out=d_xnt, in_=bufs["XNT"][:, :])
        p.dma(p.G, "yo", out=d_st, in_=bufs["STATS"][:, :])
        p.dma(p.G, "yo", out=d_bns, in_=bufs["BNS"][:, 0:16])
    v_done = p.sem("v").n
    p.Y.wait_ge(p.sem("v").h, v_done)
    yo = p.dma(p.Y, "yo", out=xo.rearrange("(t p) d -> p t d", p=128),
               in_=bufs["XRES"][:].rearrange("p (t d) -> p t d", d=1024))
    p.Y.wait_ge(p.sem("yo").h, yo)
    return p.finalize()


def ffn_host_inputs(x_shard, lt16, W1, b1, W2, b2, scale, ident):
    perm = [0, 2, 4, 6, 1, 3, 5, 7, 8, 10, 12, 14, 9, 11, 13, 15]
    return dict(
        x=np.ascontiguousarray(x_shard, dtype=np.float32),
        lt=np.ascontiguousarray(lt16[perm], dtype=np.float32),
        w1t=np.ascontiguousarray(W1.T, dtype=np.float32),
        w2t=np.ascontiguousarray(W2.T, dtype=np.float32),
        b1=np.ascontiguousarray(b1.reshape(1, -1), dtype=np.float32),
        b2=np.ascontiguousarray(b2.reshape(1, -1), dtype=np.float32),
        scl=np.broadcast_to(np.asarray(scale, np.float32).reshape(1, 1), (128, 1)).copy(),
        ident=ident,
    )


IDENT_NP = np.eye(128, dtype=BF16NP)


def emit_moe(p, bufs):
    T_, V, S, G = p.T, p.V, p.S, p.G
    XRES, XNT, CONST, IDENT = bufs["XRES"], bufs["XNT"], bufs["CONST"], bufs["IDENT"]
    PTR, PSA, PSB = bufs["PTR"], bufs["PSA"], bufs["PSB"]
    LTM, WGT, HIDT, TMPACC, M12 = bufs["LTM"], bufs["WGT"], bufs["HIDT"], bufs["TMPACC"], bufs["M12"]
    IDENTF = bufs["IDENTF"]
    rwt, w1d, w2d = bufs["rwt_dram"], bufs["w1_dram"], bufs["w2_dram"]
    RWT = bufs["RWT"]
    W1B_ = [bufs["W1A"], bufs["W1B"]]
    W2B_ = [bufs["W2A"], bufs["W2B"]]
    st, sv, ss, sgd = p.sem("t"), p.sem("v"), p.sem("s"), p.sem("gd")

    p.dma(G, "gd", out=RWT[:].rearrange("p (c e) -> p c e", e=8),
          in_=rwt.rearrange("(c p) e -> p c e", p=128))
    g_rwt = p.sem("gd").n

    def load_w(e, which, buf):
        src = (w1d if which == 1 else w2d)[e, :, :]
        return p.dma(G, "gd", out=buf[:].rearrange("p (c d) -> p c d", d=1024),
                     in_=src.rearrange("(c p) d -> p c d", p=128))

    V.tensor_copy(out=IDENTF[0:16, 0:16], in_=IDENT[0:16, 0:16])
    p.vbar()

    p.wait(T_, "v")
    T_.wait_ge(sgd.h, g_rwt)
    for c in range(DCH):
        for q in range(4):
            T_.matmul(PSA[0:8, q * 512:(q + 1) * 512], RWT[:, c * 8:(c + 1) * 8],
                      XNT[:, xnt_col(c, q * 512):xnt_col(c, (q + 1) * 512)],
                      start=(c == 0), stop=(c == DCH - 1))
    t_log = p.tick(T_, "t")
    T_.wait_ge(st.h, t_log)
    V.wait_ge(st.h, t_log)
    V.tensor_copy(out=LTM[:, 0:0 + 0] if False else bufs["LOGT"][0:8, :], in_=PSA[0:8, 0:2048])
    v_logt = p.tick(V, "v")
    T_.wait_ge(sv.h, v_logt)
    for t in range(NTT):
        T_.transpose(PSB[:, 512 + t * 8:512 + (t + 1) * 8],
                     bufs["LOGT"][0:8, t * 128:(t + 1) * 128], IDENTF[0:8, 0:8])
    t_tr = p.tick(T_, "t")
    V.wait_ge(st.h, t_tr)
    V.tensor_copy(out=LTM[:, 0:128], in_=PSB[:, 512:640])
    p.vbar()
    l3 = LTM[:, 0:128].rearrange("p (t e) -> p t e", e=8)
    m1, m2 = M12[:, 0:16], M12[:, 16:32]
    p1, p2 = M12[:, 32:48], M12[:, 48:64]
    V.tensor_reduce(out=m1, in_=l3, axis=AX.X, op=ALU.max)
    p.vbar()
    m1b = step0(m1, 8)
    m1b = bass.AP(m1b.tensor, m1b.offset, [list(m1b.ap[0]), [1, 16], [0, 8]])
    V.tensor_tensor(out=WGT[:, 0:128].rearrange("p (t e) -> p t e", e=8),
                    in0=l3, in1=m1b, op=ALU.is_equal)
    p.vbar()
    V.scalar_tensor_tensor(out=LTM[:, 128:256] if False else bufs["LM"][:, 0:128],
                           in0=WGT[:, 0:128], scalar=-1.0e30, in1=LTM[:, 0:128],
                           op0=ALU.mult, op1=ALU.add)
    p.vbar()
    V.tensor_reduce(out=m2, in_=bufs["LM"][:, 0:128].rearrange("p (t e) -> p t e", e=8),
                    axis=AX.X, op=ALU.max)
    p.vbar()
    V.tensor_tensor(out=p1, in0=m1, in1=m2, op=ALU.subtract)
    p.vbar()
    S.wait_ge(sv.h, p.sem("v").n)
    v_now = p.tick(V, "v")
    S.wait_ge(sv.h, v_now)
    S.activation(out=p1, in_=p1, func=AF.Sigmoid)
    s_sig = p.tick(S, "s")
    V.wait_ge(ss.h, s_sig)
    V.tensor_scalar(out=p2, in0=p1, scalar1=-1.0, scalar2=1.0, op0=ALU.mult, op1=ALU.add)
    p.vbar()
    m2b = step0(m2, 8)
    m2b = bass.AP(m2b.tensor, m2b.offset, [list(m2b.ap[0]), [1, 16], [0, 8]])
    p1b = bass.AP(p1.tensor, p1.offset, [list(p1.ap[0]), [1, 16], [0, 8]])
    p2b = bass.AP(p2.tensor, p2.offset, [list(p2.ap[0]), [1, 16], [0, 8]])
    V.tensor_tensor(out=bufs["LM"][:, 0:128].rearrange("p (t e) -> p t e", e=8),
                    in0=bufs["LM"][:, 0:128].rearrange("p (t e) -> p t e", e=8),
                    in1=m2b, op=ALU.is_equal)
    p.vbar()
    V.tensor_tensor(out=WGT[:, 0:128].rearrange("p (t e) -> p t e", e=8),
                    in0=WGT[:, 0:128].rearrange("p (t e) -> p t e", e=8),
                    in1=p1b, op=ALU.mult)
    V.tensor_tensor(out=bufs["LM"][:, 0:128].rearrange("p (t e) -> p t e", e=8),
                    in0=bufs["LM"][:, 0:128].rearrange("p (t e) -> p t e", e=8),
                    in1=p2b, op=ALU.mult)
    p.vbar()
    V.tensor_tensor(out=WGT[:, 0:128], in0=WGT[:, 0:128], in1=bufs["LM"][:, 0:128], op=ALU.add)
    p.vbar()
    v_wgt = p.tick(V, "v")

    w_ready = {}
    w_ready[(1, 0)] = load_w(0, 1, W1B_[0])
    w_ready[(2, 0)] = load_w(0, 2, W2B_[0])
    guards = {}
    for e in range(8):
        pb = e % 2
        if e + 1 < 8:
            G.wait_ge(st.h, guards.get(("w", 1 - pb), 0))
            w_ready[(1, e + 1)] = load_w(e + 1, 1, W1B_[1 - pb])
            w_ready[(2, e + 1)] = load_w(e + 1, 2, W2B_[1 - pb])
        T_.wait_ge(sgd.h, w_ready[(1, e)])
        for hc in range(DCH):
            for half in range(2):
                ps = PSA[:, 0:1024] if half == 0 else PSA[:, 1024:2048]
                T_.wait_ge(ss.h, guards.get(("hid", half), 0))
                for c in range(DCH):
                    T_.matmul(ps[:, 0:512],
                              W1B_[pb][:, c * 1024 + hc * 128:c * 1024 + (hc + 1) * 128],
                              XNT[:, xnt_col(c, half * 1024):xnt_col(c, half * 1024 + 512)],
                              start=(c == 0), stop=False)
                    T_.matmul(ps[:, 512:1024],
                              W1B_[pb][:, c * 1024 + hc * 128:c * 1024 + (hc + 1) * 128],
                              XNT[:, xnt_col(c, half * 1024 + 512):xnt_col(c, half * 1024 + 1024)],
                              start=(c == 0), stop=(c == DCH - 1))
                t_hid = p.tick(T_, "t")
                S.wait_ge(st.h, t_hid)
                S.activation(out=HIDT[:, hc * 2048 + half * 1024:hc * 2048 + (half + 1) * 1024],
                             in_=ps[:, 0:1024], func=AF.Relu)
                guards[("hid", half)] = p.tick(S, "s")
        s_relu = p.sem("s").n
        V.wait_ge(ss.h, s_relu)
        for hc in range(DCH):
            V.tensor_tensor(out=HIDT[:, hc * 2048:(hc + 1) * 2048],
                            in0=HIDT[:, hc * 2048:(hc + 1) * 2048],
                            in1=HIDT[:, hc * 2048:(hc + 1) * 2048], op=ALU.mult)
        v_sq = p.tick(V, "v")
        T_.wait_ge(sv.h, v_sq)
        T_.wait_ge(sgd.h, w_ready[(2, e)])
        for t in range(NTT):
            T_.wait_ge(sv.h, guards.get("out", 0))
            for hc in range(DCH):
                for o2 in range(2):
                    T_.matmul(PSB[:, o2 * 512:(o2 + 1) * 512],
                              HIDT[:, hc * 2048 + t * 128:hc * 2048 + (t + 1) * 128],
                              W2B_[pb][:, hc * 1024 + o2 * 512:hc * 1024 + (o2 + 1) * 512],
                              start=(hc == 0), stop=(hc == DCH - 1))
            t_out = p.tick(T_, "t")
            guards[("w", pb)] = t_out
            V.wait_ge(st.h, t_out)
            V.tensor_scalar(out=TMPACC[:], in0=PSB[:, 0:1024],
                            scalar1=WGT[:, t * 8 + e:t * 8 + e + 1], scalar2=None, op0=ALU.mult)
            V.tensor_tensor(out=XRES[:, t * 1024:(t + 1) * 1024],
                            in0=XRES[:, t * 1024:(t + 1) * 1024], in1=TMPACC[:], op=ALU.add)
            guards["out"] = p.tick(V, "v")
    return p.sem("v").n


def build_moe_launch():
    p = Prog("moe")
    x = p.din("x", [TLOC, D])
    rwt = p.din("rwt", [D, 8])
    w1 = p.din("w1", [8, D, D])
    w2 = p.din("w2", [8, D, D])
    ident = p.din("ident", [128, 128], BF16)
    xo = p.dout("xo", [TLOC, D])

    bufs = alloc_common(p)
    bufs.update(
        rwt_dram=rwt, w1_dram=w1, w2_dram=w2,
        RWT=p.sbuf([128, 64], BF16, "RWT"),
        LOGT=p.sbuf([8, 2048], F32, "LOGT"),
        LTM=p.sbuf([128, 128], F32, "LTM"),
        LM=p.sbuf([128, 128], F32, "LM"),
        WGT=p.sbuf([128, 128], F32, "WGT"),
        M12=p.sbuf([128, 64], F32, "M12"),
        IDENTF=p.sbuf([16, 16], F32, "IDENTF"),
        HIDT=p.sbuf([128, 8 * 2048], BF16, "HIDT"),
        W1A=p.sbuf([128, 8192], BF16, "W1A"),
        W1B=p.sbuf([128, 8192], BF16, "W1B"),
        W2A=p.sbuf([128, 8192], BF16, "W2A"),
        W2B=p.sbuf([128, 8192], BF16, "W2B"),
        TMPACC=p.sbuf([128, 1024], F32, "TMPACC"),
        PSA=p.psum([128, 2048], F32, "PSA"),
        PSB=p.psum([128, 1024], F32, "PSB"),
    )
    bufs["STAGE"] = bufs["HIDT"][:, 0:1024]
    emit_init(p, bufs, x, ident)
    emit_ln(p, bufs)
    emit_moe(p, bufs)
    v_done = p.sem("v").n
    p.Y.wait_ge(p.sem("v").h, v_done)
    yo = p.dma(p.Y, "yo", out=xo.rearrange("(t p) d -> p t d", p=128),
               in_=bufs["XRES"][:].rearrange("p (t d) -> p t d", d=1024))
    p.Y.wait_ge(p.sem("yo").h, yo)
    return p.finalize()


def moe_host_inputs(x_shard, router_w, w1_all, w2_all, ident):
    return dict(
        x=np.ascontiguousarray(x_shard, dtype=np.float32),
        rwt=np.ascontiguousarray(router_w.T, dtype=np.float32),
        w1=np.ascontiguousarray(w1_all, dtype=np.float32),
        w2=np.ascontiguousarray(w2_all, dtype=np.float32),
        ident=ident,
    )


def build_attn_a(kind):
    p = Prog(f"attn_a_{kind}")
    x = p.din("x", [TLOC, D])
    nqkv = {"s1": 1536, "s3": 3072, "s5": 3072}[kind]
    wqkv = p.din("wqkv", [D, nqkv])
    ident = p.din("ident", [128, 128], BF16)
    bufs = alloc_common(p)
    CONST = bufs["CONST"]
    PSA = p.psum([128, 2048], F32, "PSA")
    PSB = p.psum([128, 1024], F32, "PSB")
    WA = p.sbuf([128, DCH * 512], BF16, "WAw")
    WB = p.sbuf([128, DCH * 512], BF16, "WBw")
    bufs["STAGE"] = p.sbuf([128, 1024], BF16, "STAGEA")[0:128, 0:1024]
    emit_init(p, bufs, x, ident)
    T_, V, S, G, Y = p.T, p.V, p.S, p.G, p.Y
    st, sv, ss, sgd = p.sem("t"), p.sem("v"), p.sem("s"), p.sem("gd")
    XNT = bufs["XNT"]
    wqr = wqkv.rearrange("(c p) n -> p c n", p=128)

    def load_wslice(o, buf):
        return p.dma(G, "gd", out=buf[:].rearrange("p (c n) -> p c n", n=512),
                     in_=wqr[:, :, o * 512:(o + 1) * 512])

    emit_ln(p, bufs)

    if kind in ("s1", "s5"):
        HK = 4 if kind == "s1" else 16
        nck = HK * 64
        ctab = p.din("ctab", [128, 1024])
        stab = p.din("stab", [128, 1024])
        CTAB = p.sbuf([128, 1024], F32, "CTAB")
        STAB = p.sbuf([128, 1024], F32, "STAB")
        p.dma(Y, "yd", out=CTAB[:], in_=ctab)
        y_tabs = p.dma(Y, "yd", out=STAB[:], in_=stab)
        QKV = p.sbuf([128, NTT * (1024 + nck)], BF16, "QKV")
        VSTG = p.sbuf([128, 2048], BF16, "VSTG")
        SCR = p.sbuf([128, 2048], F32, "SCR")
        QS = p.sbuf([128, 64], F32, "QS")
        qk_o = p.dout("qk_o", [TLOC, 1024 + nck], BF16)
        v_o = p.dout("v_o", [TLOC, nck], BF16)
        nqk = 1024 + nck
        nos = nqkv // 512
        p.wait(T_, "v")
        wready = [load_wslice(0, WA)]
        guards = {}
        vg = {}
        for o in range(nos):
            buf = WA if o % 2 == 0 else WB
            if o + 1 < nos:
                G.wait_ge(st.h, guards.get(("w", 1 - o % 2), 0))
                wready.append(load_wslice(o + 1, WB if o % 2 == 0 else WA))
            T_.wait_ge(sgd.h, wready[o])
            for t in range(NTT):
                ps = PSA[:, (t % 4) * 512:(t % 4 + 1) * 512]
                T_.wait_ge(sv.h, vg.get(t % 4, 0))
                for c in range(DCH):
                    T_.matmul(ps, XNT[:, xnt_col(c, t * 128):xnt_col(c, (t + 1) * 128)],
                              buf[:, c * 512:(c + 1) * 512],
                              start=(c == 0), stop=(c == DCH - 1))
                t_mm = p.tick(T_, "t")
                guards[("w", o % 2)] = t_mm
                V.wait_ge(st.h, t_mm)
                g0 = o * 512
                if g0 + 512 <= nqk:
                    V.tensor_copy(out=QKV[:, t * nqk + g0:t * nqk + g0 + 512], in_=ps)
                    vg[t % 4] = p.tick(V, "v")
                elif g0 >= nqk:
                    V.tensor_copy(out=VSTG[:, (t % 2) * 1024:(t % 2) * 1024 + 512], in_=ps)
                    vg[t % 4] = vv = p.tick(V, "v")
                    Y.wait_ge(sv.h, vv)
                    p.dma(Y, "yo", out=v_o[t * 128:(t + 1) * 128, g0 - nqk:g0 - nqk + 512],
                          in_=VSTG[:, (t % 2) * 1024:(t % 2) * 1024 + 512])
                else:
                    nk = nqk - g0
                    V.tensor_copy(out=QKV[:, t * nqk + g0:t * nqk + g0 + nk], in_=ps[:, 0:nk])
                    V.tensor_copy(out=VSTG[:, (t % 2) * 1024:(t % 2) * 1024 + 512 - nk],
                                  in_=ps[:, nk:512])
                    vg[t % 4] = vv = p.tick(V, "v")
                    Y.wait_ge(sv.h, vv)
                    p.dma(Y, "yo", out=v_o[t * 128:(t + 1) * 128, 0:512 - nk],
                          in_=VSTG[:, (t % 2) * 1024:(t % 2) * 1024 + 512 - nk])
        V.wait_ge(p.sem("yd").h, y_tabs)
        for t in range(NTT):
            qv = QKV[:, t * nqk:t * nqk + 1024]
            kv = QKV[:, t * nqk + 1024:t * nqk + 1024 + nck]
            if kind == "s1":
                for (buf2, nh, qoff) in ((qv, 16, 0), (kv, 4, 16)):
                    V.tensor_tensor(out=SCR[:, 0:nh * 64], in0=buf2, in1=buf2, op=ALU.mult)
                    p.vbar()
                    V.tensor_reduce(out=QS[:, qoff:qoff + nh],
                                    in_=SCR[:, 0:nh * 64].rearrange("p (h d) -> p h d", d=64),
                                    axis=AX.X, op=ALU.add)
                    p.vbar()
                s_q = p.tick(V, "v")
                S.wait_ge(sv.h, s_q)
                S.activation(out=QS[:, 32:52], in_=QS[:, 0:20], func=AF.Sqrt)
                s_sq = p.tick(S, "s")
                V.wait_ge(ss.h, s_sq)
                V.tensor_scalar(out=QS[:, 32:52], in0=QS[:, 32:52], scalar1=1e-10,
                                scalar2=None, op0=ALU.add)
                p.vbar()
                V.reciprocal(out=QS[:, 0:20], in_=QS[:, 32:52])
                p.vbar()
                for (buf2, nh, qoff) in ((qv, 16, 0), (kv, 4, 16)):
                    rn = QS[:, qoff:qoff + nh]
                    rnb = bass.AP(rn.tensor, rn.offset, [list(rn.ap[0]), [1, nh], [0, 64]])
                    V.tensor_tensor(out=buf2.rearrange("p (h d) -> p h d", d=64),
                                    in0=buf2.rearrange("p (h d) -> p h d", d=64),
                                    in1=rnb, op=ALU.mult)
                p.vbar()
            ct, stn = CTAB[:, t * 64:(t + 1) * 64], STAB[:, t * 64:(t + 1) * 64]
            for (buf2, nh) in ((qv, 16), (kv, 4 if kind == "s1" else 16)):
                x1 = bass.AP(buf2.tensor, buf2.offset, [list(buf2.ap[0]), [64, nh], [1, 32]])
                x2 = bass.AP(buf2.tensor, buf2.offset + 32, [list(buf2.ap[0]), [64, nh], [1, 32]])
                cA = bass.AP(ct.tensor, ct.offset, [list(ct.ap[0]), [0, nh], [1, 32]])
                cB = bass.AP(ct.tensor, ct.offset + 32, [list(ct.ap[0]), [0, nh], [1, 32]])
                sA = bass.AP(stn.tensor, stn.offset, [list(stn.ap[0]), [0, nh], [1, 32]])
                sB = bass.AP(stn.tensor, stn.offset + 32, [list(stn.ap[0]), [0, nh], [1, 32]])
                u1 = SCR[:, 0:nh * 32].rearrange("p (h d) -> p h d", d=32)
                u2 = SCR[:, 512:512 + nh * 32].rearrange("p (h d) -> p h d", d=32)
                u3 = SCR[:, 1024:1024 + nh * 32].rearrange("p (h d) -> p h d", d=32)
                u4 = SCR[:, 1536:1536 + nh * 32].rearrange("p (h d) -> p h d", d=32)
                V.tensor_tensor(out=u1, in0=x1, in1=cA, op=ALU.mult)
                V.tensor_tensor(out=u2, in0=x2, in1=sA, op=ALU.mult)
                V.tensor_tensor(out=u3, in0=x1, in1=sB, op=ALU.mult)
                V.tensor_tensor(out=u4, in0=x2, in1=cB, op=ALU.mult)
                p.vbar()
                V.tensor_tensor(out=x1, in0=u1, in1=u2, op=ALU.subtract)
                V.tensor_tensor(out=x2, in0=u4, in1=u3, op=ALU.add)
                p.vbar()
        v_done = p.tick(V, "v")
        Y.wait_ge(sv.h, v_done)
        yo = p.dma(Y, "yo", out=qk_o.rearrange("(t p) n -> p t n", p=128),
                   in_=QKV[:].rearrange("p (t n) -> p t n", n=nqk))
        Y.wait_ge(p.sem("yo").h, p.sem("yo").n)
    else:
        bqkv = p.din("bqkv", [1, 3072])
        aq = p.din("aq", [64, 16])
        bq = p.din("bq", [16, 64])
        av = p.din("av", [64, 16])
        bv = p.din("bv", [16, 64])
        MISC = p.sbuf([128, 5120], BF16, "MISC")
        ONES = MISC[0:1, 0:2048]
        BQKV = MISC[0:1, 2048:5120]
        G.memset(ONES, 1.0)
        p.tick(G, "g")
        p.dma(G, "gd", out=BQKV, in_=bqkv)
        AQ = p.sbuf([128, 32], BF16, "AQsb")
        BQ = p.sbuf([16, 128], BF16, "BQsb")
        p.dma(G, "gd", out=AQ[0:64, 0:16], in_=aq)
        p.dma(G, "gd", out=AQ[64:128, 0:16], in_=aq)
        p.dma(G, "gd", out=AQ[0:64, 16:32], in_=av)
        p.dma(G, "gd", out=AQ[64:128, 16:32], in_=av)
        p.dma(G, "gd", out=BQ[0:16, 0:64], in_=bq)
        g_lw = p.dma(G, "gd", out=BQ[0:16, 64:128], in_=bv)
        QT = p.sbuf([128, DCH * 2048], BF16, "QTsb")
        VT = p.sbuf([128, DCH * 2048], BF16, "VTsb")
        KSTG = p.sbuf([128, 2048], BF16, "KSTG")
        TQ = p.sbuf([16, 2048], BF16, "TQ")
        qt_o = p.dout("qt_o", [1024, 2048], BF16)
        kt_o = p.dout("kt_o", [1024, 2048], BF16)
        vt_o = p.dout("vt_o", [1024, 2048], BF16)

        def wslice_oc(oc, buf):
            return p.dma(G, "gd", out=buf[:, 0:1024].rearrange("p (c n) -> p c n", n=128),
                         in_=wqr[:, :, oc * 128:(oc + 1) * 128])

        p.wait(T_, "v")
        T_.wait_ge(p.sem("g").h, p.sem("g").n)
        wready = [wslice_oc(0, WA)]
        guards = {}
        vg = {}
        for oc in range(24):
            buf = WA if oc % 2 == 0 else WB
            if oc + 1 < 24:
                G.wait_ge(st.h, guards.get(("w", 1 - oc % 2), 0))
                wready.append(wslice_oc(oc + 1, WB if oc % 2 == 0 else WA))
            T_.wait_ge(sgd.h, wready[oc])
            for q4 in range(4):
                ps = PSA[:, (q4 % 4) * 512:(q4 % 4 + 1) * 512]
                T_.wait_ge(sv.h, vg.get(q4 % 4, 0))
                for c in range(DCH):
                    T_.matmul(ps, buf[:, c * 128:(c + 1) * 128],
                              XNT[:, xnt_col(c, q4 * 512):xnt_col(c, (q4 + 1) * 512)],
                              start=(c == 0), stop=False)
                T_.matmul(ps, BQKV[0:1, oc * 128:(oc + 1) * 128],
                          ONES[0:1, q4 * 512:(q4 + 1) * 512], start=False, stop=True)
                t_mm = p.tick(T_, "t")
                guards[("w", oc % 2)] = t_mm
                V.wait_ge(st.h, t_mm)
                if oc < 8:
                    V.tensor_copy(out=QT[:, oc * 2048 + q4 * 512:oc * 2048 + (q4 + 1) * 512], in_=ps)
                    vg[q4 % 4] = p.tick(V, "v")
                elif oc >= 16:
                    V.tensor_copy(out=VT[:, (oc - 16) * 2048 + q4 * 512:(oc - 16) * 2048 + (q4 + 1) * 512], in_=ps)
                    vg[q4 % 4] = p.tick(V, "v")
                else:
                    V.tensor_copy(out=KSTG[:, q4 * 512:(q4 + 1) * 512], in_=ps)
                    vv = p.tick(V, "v")
                    vg[q4 % 4] = vv
                    if q4 == 3:
                        Y.wait_ge(sv.h, vv)
                        p.dma(Y, "yo", out=kt_o[(oc - 8) * 128:(oc - 7) * 128, :], in_=KSTG[:, 0:2048])
                if oc >= 8 and oc < 16 and q4 == 3:
                    V.wait_ge(p.sem("yo").h, p.sem("yo").n)
        T_.wait_ge(sgd.h, g_lw)
        psb_guard = {}
        stt_guard = 0
        for (base_sb, acol, bcol) in ((QT, 0, 0), (VT, 16, 64)):
            for h in range(16):
                ch, rb = h // 2, (h % 2) * 64
                hrows = base_sb[rb:rb + 64, ch * 2048:(ch + 1) * 2048]
                for q4 in range(4):
                    T_.wait_ge(sv.h, psb_guard.get(q4 % 2, 0))
                    T_.matmul(PSB[0:16, (q4 % 2) * 512:(q4 % 2) * 512 + 512],
                              AQ[rb:rb + 64, acol:acol + 16],
                              hrows[:, q4 * 512:(q4 + 1) * 512],
                              start=True, stop=True)
                    t_t = p.tick(T_, "t")
                    V.wait_ge(st.h, t_t)
                    V.tensor_copy(out=TQ[0:16, q4 * 512:(q4 + 1) * 512],
                                  in_=PSB[0:16, (q4 % 2) * 512:(q4 % 2) * 512 + 512])
                    psb_guard[q4 % 2] = p.tick(V, "v")
                v_tq = p.sem("v").n
                T_.wait_ge(sv.h, v_tq)
                for q4 in range(4):
                    T_.matmul(PSA[0:64, q4 * 512:(q4 + 1) * 512], BQ[0:16, bcol:bcol + 64],
                              TQ[0:16, q4 * 512:(q4 + 1) * 512], start=True, stop=True)
                t_lo = p.tick(T_, "t")
                V.wait_ge(st.h, t_lo)
                V.scalar_tensor_tensor(out=hrows, in0=PSA[0:64, 0:2048], scalar=2.0,
                                       in1=hrows, op0=ALU.mult, op1=ALU.add)
                p.vbar()
                stt_guard = p.tick(V, "v")
        v_done = p.sem("v").n
        Y.wait_ge(sv.h, v_done)
        p.dma(Y, "yo", out=qt_o.rearrange("(c p) n -> p c n", p=128),
              in_=QT[:].rearrange("p (c n) -> p c n", n=2048))
        p.dma(Y, "yo", out=vt_o.rearrange("(c p) n -> p c n", p=128),
              in_=VT[:].rearrange("p (c n) -> p c n", n=2048))
        Y.wait_ge(p.sem("yo").h, p.sem("yo").n)
    return p.finalize()


def build_attn_b(kind):
    causal = kind in ("s1", "s5")
    nkvh = 8
    p = Prog(f"attn_b_{kind}")
    qt = p.din("qt", [512, SEQ], BF16)
    kt = p.din("kt", [nkvh * 64, SEQ], BF16)
    vt = p.din("vt", [SEQ, nkvh * 65], BF16)
    aot_o = p.dout("aot", [512, SEQ], BF16)
    if causal:
        masks = p.din("masks", [128, 2048], BF16)

    QT = p.sbuf([128, 4 * SEQ], BF16, "QT")
    KT = p.sbuf([128, (nkvh * 64 // 128) * SEQ if nkvh > 2 else SEQ, ], BF16, "KT")
    VT = p.sbuf([128, 32 * (nkvh * 65)], BF16, "VT")
    PT = p.sbuf([128, 32 * 512], BF16, "PT")
    AOT = p.sbuf([128, 4 * SEQ], BF16, "AOT")
    RSB = p.sbuf([128, 512], F32, "RSB")
    AOS = p.sbuf([128, 512], BF16, "AOS")
    MSK = p.sbuf([128, 2048], BF16, "MSK") if causal else None
    CONST = p.sbuf([128, NCONST], F32, "CONST")
    ONE64 = p.sbuf([128, 64], F32, "ONE64")
    PSA = p.psum([128, 2048], F32, "PSA")
    PSB = p.psum([128, 1024], F32, "PSB")
    T_, V, S, G, Y = p.T, p.V, p.S, p.G, p.Y
    st, sv, ss, sgd, syd = p.sem("t"), p.sem("v"), p.sem("s"), p.sem("gd"), p.sem("yd")

    emit_consts(p, CONST)
    G.memset(ONE64[64:65, 0:64], 1.0)
    p.tick(p.G, "g")
    p.nc.const_aps.register(F32, 0.0, CONST[:, C_ZERO:C_ZERO + 1])
    p.nc.const_aps.register(F32, 1.0, CONST[:, C_ONE:C_ONE + 1])
    p.dma(Y, "yd", out=QT[:].rearrange("p (c n) -> p c n", n=SEQ),
          in_=qt.rearrange("(c p) n -> p c n", p=128))
    p.dma(Y, "yd", out=KT[:, 0:(nkvh * 64 // 128) * SEQ if nkvh > 2 else SEQ]
          .rearrange("p (c n) -> p c n", n=SEQ) if nkvh > 2 else KT[:, 0:SEQ]
          .rearrange("p (c n) -> p c n", n=SEQ),
          in_=kt.rearrange("(c p) n -> p c n", p=128))
    p.dma(Y, "yd", out=VT[:].rearrange("p (t n) -> p t n", n=nkvh * 65),
          in_=vt.rearrange("(t p) n -> p t n", p=128))
    if causal:
        p.dma(Y, "yd", out=MSK[:], in_=masks)
    y_in = p.sem("yd").n
    for rec in (T_, V, S):
        rec.wait_ge(p.sem("g").h, p.sem("g").n)
        rec.wait_ge(syd.h, y_in)

    guards = {}
    for h in range(8):
        hl = h
        kchunk, kbase = hl // 2, (hl % 2) * 64
        for qt_i in range(8):
            live = (4 * qt_i + 4) if causal else 32
            for g0 in range(0, live, 2):
                nch = min(2, live - g0)
                for j in range(nch):
                    kc = g0 + j
                    ps = PSA[:, ((g0 // 2 + j // 2 * 0) % 4) * 512 + j * 0:0] if False else \
                        PSA[:, (((g0 // 2) % 2) * 2 + j) * 512:(((g0 // 2) % 2) * 2 + j + 1) * 512]
                    T_.wait_ge(ss.h, guards.get(("exp", (g0 // 2) % 2), 0))
                    T_.matmul(ps,
                              KT[kbase:kbase + 64, kchunk * SEQ + kc * 128:kchunk * SEQ + (kc + 1) * 128],
                              QT[(h % 2) * 64:(h % 2) * 64 + 64,
                                 (h // 2) * SEQ + qt_i * 512:(h // 2) * SEQ + (qt_i + 1) * 512],
                              start=True, stop=True)
                t_qk = p.tick(T_, "t")
                S.wait_ge(st.h, t_qk)
                S.activation(out=PT[:, g0 * 512:(g0 + nch) * 512],
                             in_=PSA[:, (((g0 // 2) % 2) * 2) * 512:(((g0 // 2) % 2) * 2 + nch) * 512],
                             func=AF.Exp, bias=CONST[:, C_NEGEXPOFF:C_NEGEXPOFF + 1],
                             scale=CONST[:, C_SCL8:C_SCL8 + 1])
                guards[("exp", (g0 // 2) % 2)] = p.tick(S, "s")
            s_exp = p.sem("s").n
            if causal:
                V.wait_ge(ss.h, s_exp)
                for j in range(4):
                    kc = 4 * qt_i + j
                    V.tensor_tensor(out=PT[:, kc * 512:(kc + 1) * 512],
                                    in0=PT[:, kc * 512:(kc + 1) * 512],
                                    in1=MSK[:, j * 512:(j + 1) * 512], op=ALU.mult)
                v_msk = p.tick(V, "v")
                T_.wait_ge(sv.h, v_msk)
            T_.wait_ge(ss.h, s_exp)
            T_.wait_ge(sv.h, guards.get("pv", 0))
            for kc in range(live):
                T_.matmul(PSB[0:65, 0:512],
                          VT[:, kc * (nkvh * 65) + hl * 65:kc * (nkvh * 65) + (hl + 1) * 65],
                          PT[:, kc * 512:(kc + 1) * 512],
                          start=(kc == 0), stop=(kc == live - 1))
            t_pv = p.tick(T_, "t")
            S.wait_ge(st.h, t_pv)
            S.activation(out=RSB[64:65, 0:512], in_=PSB[64:65, 0:512], func=AF.Ln)
            S.activation(out=RSB[64:65, 0:512], in_=RSB[64:65, 0:512], func=AF.Exp,
                         scale=CONST[64:65, C_NEG1:C_NEG1 + 1], bias=CONST[64:65, C_ZERO:C_ZERO + 1])
            S.activation(out=AOS[0:64, 0:512], in_=PSB[0:64, 0:512], func=AF.Copy)
            s_rs = p.tick(S, "s")
            T_.wait_ge(ss.h, s_rs)
            T_.matmul(PSB[0:64, 512:1024], ONE64[64:65, 0:64], RSB[64:65, 0:512],
                      start=True, stop=True)
            t_bc = p.tick(T_, "t")
            V.wait_ge(st.h, t_bc)
            V.tensor_tensor(out=AOT[(h % 2) * 64:(h % 2) * 64 + 64,
                                    (h // 2) * SEQ + qt_i * 512:(h // 2) * SEQ + (qt_i + 1) * 512],
                            in0=AOS[0:64, 0:512], in1=PSB[0:64, 512:1024], op=ALU.mult)
            guards["pv"] = p.tick(V, "v")
    v_done = p.sem("v").n
    Y.wait_ge(sv.h, v_done)
    yo = p.dma(Y, "yo", out=aot_o.rearrange("(c p) n -> p c n", p=128),
               in_=AOT[:].rearrange("p (c n) -> p c n", n=SEQ))
    Y.wait_ge(p.sem("yo").h, yo)
    return p.finalize()


def build_attn_c(with_bias, gate=False):
    p = Prog("attn_c")
    x = p.din("x", [TLOC, D])
    aot = p.din("aot", [D, TLOC], BF16)
    wproj = p.din("wproj", [D, D])
    gt = p.din("gt", [D, TLOC], BF16) if gate else None
    if with_bias:
        bproj = p.din("bproj", [1, D])
    XRES = p.sbuf([128, NTT * 1024], F32, "XRES")
    AOT = p.sbuf([128, DCH * 2048], BF16, "AOT")
    WP = p.sbuf([128, DCH * 1024], BF16, "WP")
    CONST = p.sbuf([128, NCONST], F32, "CONST")
    MISC = p.sbuf([128, 3072], BF16, "MISC2")
    ONES = MISC[0:1, 0:2048]
    BPR = MISC[0:1, 2048:3072]
    PSA = p.psum([128, 2048], F32, "PSA")
    T_, V, S, G, Y = p.T, p.V, p.S, p.G, p.Y
    st, sv, sgd, syd = p.sem("t"), p.sem("v"), p.sem("gd"), p.sem("yd")
    emit_consts(p, CONST)
    G.memset(ONES, 1.0)
    p.tick(G, "g")
    p.nc.const_aps.register(F32, 0.0, CONST[:, C_ZERO:C_ZERO + 1])
    p.nc.const_aps.register(F32, 1.0, CONST[:, C_ONE:C_ONE + 1])
    p.dma(Y, "yd", out=XRES[:].rearrange("p (t d) -> p t d", d=1024),
          in_=x.rearrange("(t p) d -> p t d", p=128))
    y_x = p.dma(Y, "yd", out=AOT[:].rearrange("p (c n) -> p c n", n=2048),
                in_=aot.rearrange("(c p) n -> p c n", p=128))
    if gate:
        GT = p.sbuf([128, DCH * 2048], BF16, "GT")
        y_x = p.dma(Y, "yd", out=GT[:].rearrange("p (c n) -> p c n", n=2048),
                    in_=gt.rearrange("(c p) n -> p c n", p=128))
    g_w = p.dma(G, "gd", out=WP[:].rearrange("p (c n) -> p c n", n=1024),
                in_=wproj.rearrange("(c p) n -> p c n", p=128))
    if with_bias:
        g_w = p.dma(G, "gd", out=BPR, in_=bproj)
    for rec in (T_, V):
        rec.wait_ge(p.sem("g").h, p.sem("g").n)
        rec.wait_ge(syd.h, y_x)
    T_.wait_ge(sgd.h, g_w)
    if gate:
        V.wait_ge(syd.h, y_x)
        for c in range(DCH):
            V.tensor_tensor(out=AOT[:, c * 2048:(c + 1) * 2048],
                            in0=AOT[:, c * 2048:(c + 1) * 2048],
                            in1=GT[:, c * 2048:(c + 1) * 2048], op=ALU.mult)
        v_gate = p.tick(V, "v")
        T_.wait_ge(p.sem("v").h, v_gate)
    guards = {}
    for t in range(NTT):
        ps = PSA[:, (t % 2) * 1024:(t % 2 + 1) * 1024]
        T_.wait_ge(sv.h, guards.get(t % 2, 0))
        for o2 in range(2):
            for c in range(DCH):
                T_.matmul(ps[:, o2 * 512:(o2 + 1) * 512],
                          AOT[:, c * 2048 + t * 128:c * 2048 + (t + 1) * 128],
                          WP[:, c * 1024 + o2 * 512:c * 1024 + (o2 + 1) * 512],
                          start=(c == 0), stop=(c == DCH - 1 and not with_bias))
            if with_bias:
                T_.matmul(ps[:, o2 * 512:(o2 + 1) * 512], ONES[0:1, 0:128],
                          BPR[0:1, o2 * 512:(o2 + 1) * 512], start=False, stop=True)
        t_mm = p.tick(T_, "t")
        V.wait_ge(st.h, t_mm)
        V.tensor_tensor(out=XRES[:, t * 1024:(t + 1) * 1024],
                        in0=XRES[:, t * 1024:(t + 1) * 1024], in1=ps[:, 0:1024], op=ALU.add)
        guards[t % 2] = p.tick(V, "v")
    xo = p.dout("xo", [TLOC, D])
    v_done = p.sem("v").n
    Y.wait_ge(sv.h, v_done)
    yo = p.dma(Y, "yo", out=xo.rearrange("(t p) d -> p t d", p=128),
               in_=XRES[:].rearrange("p (t d) -> p t d", d=1024))
    Y.wait_ge(p.sem("yo").h, yo)
    return p.finalize()


def rope_tables(kind, tok0, ntok=TLOC):
    half = 32
    theta = 1.0 / 10000.0 ** (np.arange(half, dtype=np.float64) / half)
    pos = np.arange(tok0, tok0 + ntok, dtype=np.float64)
    ang = pos[:, None] * theta[None, :]
    cos, sin = np.cos(ang), np.sin(ang)
    if kind == "s1":
        cA, cB, sA, sB = cos, cos, sin, sin
    else:
        cA = np.repeat(cos[:, 0:16], 2, axis=1)
        cB = np.repeat(cos[:, 16:32], 2, axis=1)
        sA = np.repeat(sin[:, 0:16], 2, axis=1)
        sB = np.repeat(sin[:, 16:32], 2, axis=1)
    ct = np.concatenate([cA, cB], axis=1).astype(np.float32)
    st = np.concatenate([sA, sB], axis=1).astype(np.float32)
    nt = ntok // 128
    ct = ct.reshape(nt, 128, 64).transpose(1, 0, 2).reshape(128, nt * 64)
    st = st.reshape(nt, 128, 64).transpose(1, 0, 2).reshape(128, nt * 64)
    return np.ascontiguousarray(ct), np.ascontiguousarray(st)


def build_wkv_a():
    p = Prog("wkv_a")
    x = p.din("x", [TLOC, D])
    ident = p.din("ident", [128, 128], BF16)
    bufs = alloc_common(p)
    bufs["STAGE"] = p.sbuf([128, 1024], BF16, "STAGEA")[0:128, 0:1024]
    emit_init(p, bufs, x, ident)
    emit_ln(p, bufs)
    xnt_o = p.dout("xnt", [D, TLOC], BF16)
    v_done = p.sem("v").n
    p.Y.wait_ge(p.sem("v").h, v_done)
    XNT = bufs["XNT"]
    yo = p.dma(p.Y, "yo", out=xnt_o.rearrange("(c p) n -> p c n", p=128),
               in_=XNT[:].rearrange("p (cc n) -> p cc n", n=XSTR)[:, :, 1:2049])
    p.Y.wait_ge(p.sem("yo").h, yo)
    return p.finalize()


def build_wkv_b():
    p = Prog("wkv_b")
    xnt = p.din("xnt", [D, TLOC], BF16)
    bd = p.din("bd", [128, 8], BF16)
    wk = p.din("wk", [D, D])
    wv = p.din("wv", [D, D])
    wr = p.din("wr", [D, D])
    tmk = p.din("tmk", [128, 8])
    tmv = p.din("tmv", [128, 8])
    tmr = p.din("tmr", [128, 8])
    kt_o = p.dout("kt_o", [D, TLOC], BF16)
    vt_o = p.dout("vt_o", [D, TLOC], BF16)
    srt_o = p.dout("srt_o", [D, TLOC], BF16)

    XNT = p.sbuf([128, DCH * XSTR], BF16, "XNT")
    TM = p.sbuf([128, 32], F32, "TM")
    WKA = p.sbuf([128, DCH * 1024], BF16, "WKA")
    WKB = p.sbuf([128, DCH * 1024], BF16, "WKB")
    WVA = p.sbuf([128, DCH * 1024], BF16, "WVA")
    WVB = p.sbuf([128, DCH * 1024], BF16, "WVB")
    WRA = p.sbuf([128, DCH * 1024], BF16, "WRA")
    WRB = p.sbuf([128, DCH * 1024], BF16, "WRB")
    STG = p.sbuf([128, 2048], BF16, "STG")
    STG2 = p.sbuf([128, 2048], BF16, "STG2")
    PSA = p.psum([128, 2048], F32, "PSA")
    T_, V, S, G, Y = p.T, p.V, p.S, p.G, p.Y
    st, sv, ss, sgd, syd = p.sem("t"), p.sem("v"), p.sem("s"), p.sem("gd"), p.sem("yd")

    p.dma(Y, "yd", out=XNT[:].rearrange("p (cc n) -> p cc n", n=XSTR)[:, :, 1:2049],
          in_=xnt.rearrange("(c p) n -> p c n", p=128))
    p.dma(Y, "yd", out=TM[:, 0:8], in_=tmk)
    p.dma(Y, "yd", out=TM[:, 8:16], in_=tmv)
    y_tm = p.dma(Y, "yd", out=TM[:, 16:24], in_=tmr)
    g_k = p.dma(G, "gd", out=WKA[:].rearrange("p (c n) -> p c n", n=1024),
                in_=wk.rearrange("(c p) n -> p c n", p=128))
    g_v = p.dma(G, "gd", out=WVA[:].rearrange("p (c n) -> p c n", n=1024),
                in_=wv.rearrange("(c p) n -> p c n", p=128))
    g_r = p.dma(G, "gd", out=WRA[:].rearrange("p (c n) -> p c n", n=1024),
                in_=wr.rearrange("(c p) n -> p c n", p=128))

    BD = p.sbuf([128, 8], BF16, "BD")
    ybd = p.dma(Y, "yd", out=BD[:], in_=bd)
    V.wait_ge(syd.h, p.sem("yd").n)
    for c in range(DCH):
        V.tensor_copy(out=XNT[:, c * XSTR:c * XSTR + 1], in_=BD[:, c:c + 1])
    v_bd = p.tick(V, "v")

    V.wait_ge(p.sem("gd").h, p.sem("gd").n)
    for (WA_, WB_, col, ncols) in ((WKA, WKB, 0, 1024), (WVA, WVB, 8, 1024), (WRA, WRB, 16, 1024)):
        for c in range(DCH):
            V.tensor_scalar(out=WB_[:, c * ncols:(c + 1) * ncols],
                            in0=WA_[:, c * ncols:(c + 1) * ncols],
                            scalar1=TM[:, col + c:col + c + 1], scalar2=None, op0=ALU.mult)
    p.vbar()
    for (WA_, WB_, ncols) in ((WKA, WKB, 1024), (WVA, WVB, 1024), (WRA, WRB, 1024)):
        for c in range(DCH):
            V.tensor_tensor(out=WA_[:, c * ncols:(c + 1) * ncols],
                            in0=WA_[:, c * ncols:(c + 1) * ncols],
                            in1=WB_[:, c * ncols:(c + 1) * ncols], op=ALU.subtract)
    p.vbar()
    v_w = p.tick(V, "v")

    T_.wait_ge(sv.h, v_w)
    last_evac = None

    def do_mm(WB_, WA_, ncols, oc, q4, ps):
        for c in range(DCH):
            T_.matmul(ps, WB_[:, c * ncols + oc * 128:c * ncols + (oc + 1) * 128],
                      XNT[:, xnt_col(c, q4 * 512):xnt_col(c, (q4 + 1) * 512)],
                      start=(c == 0), stop=False)
        for c in range(DCH):
            T_.matmul(ps, WA_[:, c * ncols + oc * 128:c * ncols + (oc + 1) * 128],
                      XNT[:, xnt_col(c, q4 * 512) - 1:xnt_col(c, (q4 + 1) * 512) - 1],
                      start=False, stop=(c == DCH - 1))

    for (WB_, WA_, ncols, out_d, sig) in ((WKB, WKA, 1024, kt_o, False),
                                          (WVB, WVA, 1024, vt_o, False),
                                          (WRB, WRA, 1024, srt_o, True)):
        for oc in range(ncols // 128):
            stg = STG if oc % 2 == 0 else STG2
            if last_evac is not None:
                T_.wait_ge(p.sem(last_evac[0]).h, last_evac[1])
            for q4 in range(4):
                do_mm(WB_, WA_, ncols, oc, q4, PSA[:, q4 * 512:(q4 + 1) * 512])
            t_mm = p.tick(T_, "t")
            if sig:
                S.wait_ge(st.h, t_mm)
                S.wait_ge(p.sem("yo").h, p.sem("yo").n)
                S.activation(out=stg[:, 0:2048], in_=PSA[:, 0:2048], func=AF.Sigmoid)
                last_evac = ("s", p.tick(S, "s"))
                Y.wait_ge(ss.h, last_evac[1])
            else:
                V.wait_ge(st.h, t_mm)
                V.wait_ge(p.sem("yo").h, p.sem("yo").n)
                V.tensor_copy(out=stg[:, 0:2048], in_=PSA[:, 0:2048])
                last_evac = ("v", p.tick(V, "v"))
                Y.wait_ge(sv.h, last_evac[1])
            p.dma(Y, "yo", out=out_d[oc * 128:(oc + 1) * 128, :], in_=stg[:, 0:2048])
        Y.wait_ge(p.sem("yo").h, p.sem("yo").n)
    return p.finalize()


def build_wkv_c():
    p = Prog("wkv_c")
    kt = p.din("kt", [512, SEQ], BF16)
    vt = p.din("vt", [512, SEQ], BF16)
    td = p.din("td", [128, 4])
    tf = p.din("tf", [128, 4])
    y_o = p.dout("y_o", [512, SEQ], BF16)

    KCH = p.sbuf([128, 4 * SEQ], BF16, "KCH")
    VCH = p.sbuf([128, 4 * SEQ], BF16, "VCH")
    YST = p.sbuf([128, SEQ], BF16, "YST")
    ZB = p.sbuf([128, SEQ], F32, "ZB")
    ZV = p.sbuf([128, SEQ], F32, "ZV")
    DECB = p.sbuf([128, SEQ], F32, "DECB")
    DEN = p.sbuf([128, SEQ], F32, "DEN")
    ABUF = p.sbuf([128, SEQ + 1], F32, "ABUF")
    BBUF = p.sbuf([128, SEQ + 1], F32, "BBUF")
    SM = p.sbuf([128, 16], F32, "SM")
    CONST = p.sbuf([128, NCONST], F32, "CONST")
    T_, V, S, G, Y = p.T, p.V, p.S, p.G, p.Y
    st, sv, ss, sgd, syd = p.sem("t"), p.sem("v"), p.sem("s"), p.sem("gd"), p.sem("yd")

    emit_consts(p, CONST)
    p.nc.const_aps.register(F32, 0.0, CONST[:, C_ZERO:C_ZERO + 1])
    p.nc.const_aps.register(F32, 1.0, CONST[:, C_ONE:C_ONE + 1])
    p.dma(Y, "yd", out=KCH[:].rearrange("p (c n) -> p c n", n=SEQ),
          in_=kt.rearrange("(c p) n -> p c n", p=128))
    p.dma(Y, "yd", out=VCH[:].rearrange("p (c n) -> p c n", n=SEQ),
          in_=vt.rearrange("(c p) n -> p c n", p=128))
    p.dma(Y, "yd", out=SM[:, 0:4], in_=td)
    y_in = p.dma(Y, "yd", out=SM[:, 4:8], in_=tf)
    for rec in (T_, V, S):
        rec.wait_ge(p.sem("g").h, p.sem("g").n)
        rec.wait_ge(syd.h, y_in)
    S.activation(out=SM[:, 8:12], in_=SM[:, 0:4], func=AF.Exp)
    p.sbar()
    S.activation(out=SM[:, 8:12], in_=SM[:, 8:12], func=AF.Exp,
                 scale=CONST[:, C_NEG1:C_NEG1 + 1])
    p.sbar()
    S.activation(out=SM[:, 12:16], in_=SM[:, 4:8], func=AF.Exp)
    s_pre = p.tick(S, "s")
    V.wait_ge(ss.h, s_pre)
    for cc in range(4):
        kch = KCH[:, cc * SEQ:(cc + 1) * SEQ]
        vch = VCH[:, cc * SEQ:(cc + 1) * SEQ]
        V.tensor_reduce(out=SM[:, 14 + 0:15] if False else DEN[:, 0:1], in_=kch, axis=AX.X, op=ALU.max)
        p.vbar()
        V.tensor_scalar(out=DEN[:, 1:2], in0=DEN[:, 0:1], scalar1=-1.0, scalar2=None, op0=ALU.mult)
        v_mu = p.tick(V, "v")
        S.wait_ge(sv.h, v_mu)
        S.activation(out=ZB[:], in_=kch, func=AF.Exp, bias=DEN[:, 1:2])
        s_z = p.tick(S, "s")
        V.wait_ge(ss.h, s_z)
        V.tensor_tensor(out=ZV[:], in0=ZB[:], in1=vch, op=ALU.mult)
        V.tensor_scalar(out=DECB[:], in0=ZB[:], scalar1=0.0,
                        scalar2=SM[:, 8 + cc:9 + cc], op0=ALU.mult, op1=ALU.add)
        V.memset(ABUF[:, 0:1], 0.0)
        V.memset(BBUF[:, 0:1], 0.0)
        p.vbar()
        V.tensor_tensor_scan(out=ABUF[:, 1:SEQ + 1], data0=DECB[:], data1=ZV[:],
                             initial=0.0, op0=ALU.mult, op1=ALU.add)
        V.tensor_tensor_scan(out=BBUF[:, 1:SEQ + 1], data0=DECB[:], data1=ZB[:],
                             initial=0.0, op0=ALU.mult, op1=ALU.add)
        p.vbar()
        V.scalar_tensor_tensor(out=DECB[:], in0=ZV[:], scalar=SM[:, 12 + cc:13 + cc],
                               in1=ABUF[:, 0:SEQ], op0=ALU.mult, op1=ALU.add)
        V.scalar_tensor_tensor(out=DEN[:], in0=ZB[:], scalar=SM[:, 12 + cc:13 + cc],
                               in1=BBUF[:, 0:SEQ], op0=ALU.mult, op1=ALU.add)
        p.vbar()
        V.tensor_scalar(out=DEN[:], in0=DEN[:], scalar1=1e-30, scalar2=None, op0=ALU.max)
        v_nd = p.tick(V, "v")
        S.wait_ge(sv.h, v_nd)
        S.activation(out=DEN[:], in_=DEN[:], func=AF.Ln)
        p.sbar()
        S.activation(out=DEN[:], in_=DEN[:], func=AF.Exp, scale=CONST[:, C_NEG1:C_NEG1 + 1])
        s_rd = p.tick(S, "s")
        V.wait_ge(ss.h, s_rd)
        V.tensor_tensor(out=YST[:], in0=DECB[:], in1=DEN[:], op=ALU.mult)
        v_y = p.tick(V, "v")
        Y.wait_ge(sv.h, v_y)
        p.dma(Y, "yo", out=y_o[cc * 128:(cc + 1) * 128, :], in_=YST[:])
        V.wait_ge(p.sem("yo").h, p.sem("yo").n)
    Y.wait_ge(p.sem("yo").h, p.sem("yo").n)
    return p.finalize()


def build_lt():
    p = Prog("lt")
    xlt = p.din("xlt", [16, D])
    lt_o = p.dout("lt_o", [16, D])
    X = p.sbuf([16, 1024], F32, "Xlt")
    ST = p.sbuf([16, 64], F32, "STlt")
    CONST = p.sbuf([128, NCONST], F32, "CONSTlt")
    T_, V, S, G, Y = p.T, p.V, p.S, p.G, p.Y
    emit_consts(p, CONST)
    p.nc.const_aps.register(F32, 0.0, CONST[:, C_ZERO:C_ZERO + 1])
    p.nc.const_aps.register(F32, 1.0, CONST[:, C_ONE:C_ONE + 1])
    y_x = p.dma(Y, "yd", out=X[:], in_=xlt)
    V.wait_ge(p.sem("g").h, p.sem("g").n)
    V.wait_ge(p.sem("yd").h, y_x)
    V.bn_stats(out=ST[:, 0:6], in_=X[:, 0:512])
    V.bn_stats(out=ST[:, 6:12], in_=X[:, 512:1024])
    p.vbar()
    bb = ST[:, :]
    means_v = bass.AP(bb.tensor, bb.offset + 1, [list(bb.ap[0]), [3, 4]])
    m2_v = bass.AP(bb.tensor, bb.offset + 2, [list(bb.ap[0]), [3, 4]])
    V.tensor_reduce(out=ST[:, 12:13], in_=means_v, axis=AX.X, op=ALU.add)
    V.tensor_reduce(out=ST[:, 13:14], in_=m2_v, axis=AX.X, op=ALU.add)
    V.tensor_tensor(out=ST[:, 16:20], in0=means_v, in1=means_v, op=ALU.mult)
    p.vbar()
    V.tensor_reduce(out=ST[:, 14:15], in_=ST[:, 16:20], axis=AX.X, op=ALU.add)
    p.vbar()
    V.tensor_scalar(out=ST[:, 12:13], in0=ST[:, 12:13], scalar1=0.25, scalar2=None, op0=ALU.mult)
    V.tensor_scalar(out=ST[:, 13:14], in0=ST[:, 13:14],
                    scalar1=CONST[0:16, C_INV1024:C_INV1024 + 1], scalar2=None, op0=ALU.mult)
    V.tensor_scalar(out=ST[:, 14:15], in0=ST[:, 14:15], scalar1=0.25, scalar2=None, op0=ALU.mult)
    p.vbar()
    V.tensor_tensor(out=ST[:, 15:16], in0=ST[:, 12:13], in1=ST[:, 12:13], op=ALU.mult)
    V.tensor_tensor(out=ST[:, 13:14], in0=ST[:, 13:14], in1=ST[:, 14:15], op=ALU.add)
    p.vbar()
    V.tensor_tensor(out=ST[:, 13:14], in0=ST[:, 13:14], in1=ST[:, 15:16], op=ALU.subtract)
    v_var = p.tick(V, "v")
    S.wait_ge(p.sem("v").h, v_var)
    S.activation(out=ST[:, 14:15], in_=ST[:, 13:14], func=AF.Sqrt,
                 bias=CONST[0:16, C_EPS:C_EPS + 1])
    s_std = p.tick(S, "s")
    V.wait_ge(p.sem("s").h, s_std)
    V.reciprocal(out=ST[:, 15:16], in_=ST[:, 14:15])
    p.vbar()
    V.tensor_scalar(out=X[:], in0=X[:], scalar1=ST[:, 12:13], scalar2=ST[:, 15:16],
                    op0=ALU.subtract, op1=ALU.mult)
    v_done = p.tick(V, "v")
    Y.wait_ge(p.sem("v").h, v_done)
    yo = p.dma(Y, "yo", out=lt_o, in_=X[:])
    Y.wait_ge(p.sem("yo").h, yo)
    return p.finalize()


_PROGS = {}


def _prog(name, fn, *a):
    if name not in _PROGS:
        _PROGS[name] = fn(*a)
    return _PROGS[name]


def _run(nc, ims):
    return run_bass_kernel_spmd(nc, ims, core_ids=list(range(NCORES)))


def _np(a):
    return np.asarray(a)


def _attn_host_masks():
    kk = np.arange(128)[:, None]
    qq = np.arange(512)[None, :]
    m = np.zeros((128, 2048), BF16NP)
    for j in range(4):
        m[:, j * 512:(j + 1) * 512] = (kk <= qq - 128 * j).astype(BF16NP)
    return m


def _sub_attn(x8, kind, wqkv, wproj, bias_qkv=None, lora=None, bproj=None):
    masks = _attn_host_masks()
    if kind in ("s1", "s5"):
        nc_a = _prog(f"a_{kind}", build_attn_a, kind)
        ims = []
        for c in range(NCORES):
            tok0 = (c % 2) * 2048
            ct, stt = rope_tables(kind, tok0)
            ims.append(dict(x=x8[c], wqkv=wqkv, ident=IDENT_NP, ctab=ct, stab=stt))
        ra = _run(nc_a, ims).results
        HK = 4 if kind == "s1" else 16
        nck = HK * 64
        ims_b = []
        for c in range(NCORES):
            b = c // 2
            qk = np.concatenate([_np(ra[2 * b]["qk_o"]), _np(ra[2 * b + 1]["qk_o"])], axis=0)
            vv = np.concatenate([_np(ra[2 * b]["v_o"]), _np(ra[2 * b + 1]["v_o"])], axis=0)
            qT = np.ascontiguousarray(qk[:, 0:1024].T)
            krep = qk[:, 1024:1024 + nck].reshape(SEQ, HK, 64)
            vrep = vv.reshape(SEQ, HK, 64)
            if HK == 4:
                krep = np.repeat(krep, 4, axis=1)
                vrep = np.repeat(vrep, 4, axis=1)
            kT = np.ascontiguousarray(krep.reshape(SEQ, 1024).T)
            ones = np.ones((SEQ, 1), BF16NP)
            vst = []
            for hk in range(16):
                vst += [vrep[:, hk], ones]
            v_ext = np.concatenate(vst, axis=1)
            hh = c % 2
            ims_b.append(dict(qt=np.ascontiguousarray(qT[hh * 512:(hh + 1) * 512]),
                              kt=np.ascontiguousarray(kT[hh * 512:(hh + 1) * 512]),
                              vt=np.ascontiguousarray(v_ext[:, hh * 520:(hh + 1) * 520]),
                              masks=masks))
    else:
        nc_a = _prog("a_s3", build_attn_a, "s3")
        aq, bq, av, bv = lora
        ims = [dict(x=x8[c], wqkv=wqkv, ident=IDENT_NP, bqkv=bias_qkv.reshape(1, -1),
                    aq=aq, bq=bq, av=av, bv=bv) for c in range(NCORES)]
        ra = _run(nc_a, ims).results
        ims_b = []
        for c in range(NCORES):
            b = c // 2
            qT = np.concatenate([_np(ra[2 * b]["qt_o"]), _np(ra[2 * b + 1]["qt_o"])], axis=1)
            kT = np.concatenate([_np(ra[2 * b]["kt_o"]), _np(ra[2 * b + 1]["kt_o"])], axis=1)
            vT = np.concatenate([_np(ra[2 * b]["vt_o"]), _np(ra[2 * b + 1]["vt_o"])], axis=1)
            vtm = np.ascontiguousarray(vT.T)
            ones = np.ones((SEQ, 1), BF16NP)
            vst = []
            for hk in range(16):
                vst += [vtm[:, hk * 64:(hk + 1) * 64], ones]
            v_ext = np.concatenate(vst, axis=1)
            hh = c % 2
            ims_b.append(dict(qt=np.ascontiguousarray(qT[hh * 512:(hh + 1) * 512]),
                              kt=np.ascontiguousarray(kT[hh * 512:(hh + 1) * 512]),
                              vt=np.ascontiguousarray(v_ext[:, hh * 520:(hh + 1) * 520])))
    nc_b = _prog(f"b_{'c' if kind in ('s1', 's5') else 'f'}", build_attn_b,
                 "s1" if kind in ("s1", "s5") else "s3")
    rb = _run(nc_b, ims_b).results
    with_bias = bproj is not None
    nc_c = _prog(f"c_{with_bias}", build_attn_c, with_bias)
    ims_c = []
    for c in range(NCORES):
        b = c // 2
        aot = np.concatenate([_np(rb[2 * b]["aot"]), _np(rb[2 * b + 1]["aot"])], axis=0)
        im = dict(x=x8[c], aot=np.ascontiguousarray(aot[:, (c % 2) * 2048:(c % 2 + 1) * 2048]),
                  wproj=wproj)
        if with_bias:
            im["bproj"] = bproj.reshape(1, -1)
        ims_c.append(im)
    rc_ = _run(nc_c, ims_c).results
    return [np.ascontiguousarray(_np(rc_[c]["xo"])) for c in range(NCORES)]


def _sub_ffn(x8, W1, b1, W2, b2, scale):
    nc_lt = _prog("lt", build_lt)
    ims_lt = [dict(xlt=np.ascontiguousarray(x8[(c // 2) * 2][1:17])) for c in range(NCORES)]
    rlt = _run(nc_lt, ims_lt).results
    nc = _prog("ffn", build_ffn_launch)
    ims = []
    for c in range(NCORES):
        lt16 = _np(rlt[(c // 2) * 2]["lt_o"])
        ims.append(ffn_host_inputs(x8[c], lt16, W1, b1, W2, b2, scale, IDENT_NP))
    r = _run(nc, ims).results
    return [np.ascontiguousarray(_np(r[c]["xo"])) for c in range(NCORES)]


def _sub_moe(x8, router_w, w1_all, w2_all):
    nc = _prog("moe", build_moe_launch)
    ims = [moe_host_inputs(x8[c], router_w, w1_all, w2_all, IDENT_NP) for c in range(NCORES)]
    r = _run(nc, ims).results
    return [np.ascontiguousarray(_np(r[c]["xo"])) for c in range(NCORES)]


def _sub_wkv(x8, td, tfv, tmk, tmv, tmr, Wk, Wv, Wr, Wo):
    fm = lambda a: np.ascontiguousarray(a.reshape(8, 128).T.astype(np.float32))
    fm4 = lambda a: np.ascontiguousarray(a.reshape(4, 128).T.astype(np.float32))
    nc_a = _prog("wkv_a", build_wkv_a)
    ra = _run(nc_a, [dict(x=x8[c], ident=IDENT_NP) for c in range(NCORES)]).results
    nc_b = _prog("wkv_b", build_wkv_b)
    zbd = np.zeros((128, 8), BF16NP)
    wkT = np.ascontiguousarray(Wk.T)
    wvT = np.ascontiguousarray(Wv.T)
    wrT = np.ascontiguousarray(Wr.T)
    ims_b = []
    for c in range(NCORES):
        b = c // 2
        if c % 2 == 0:
            bd = zbd
        else:
            bd = np.ascontiguousarray(_np(ra[2 * b]["xnt"])[:, 2047].reshape(8, 128).T)
        ims_b.append(dict(xnt=_np(ra[c]["xnt"]), bd=bd, wk=wkT, wv=wvT, wr=wrT,
                          tmk=fm(tmk), tmv=fm(tmv), tmr=fm(tmr)))
    rb = _run(nc_b, ims_b).results
    nc_c = _prog("wkv_c", build_wkv_c)
    ims_c = []
    for c in range(NCORES):
        b = c // 2
        ktf = np.concatenate([_np(rb[2 * b]["kt_o"]), _np(rb[2 * b + 1]["kt_o"])], axis=1)
        vtf = np.concatenate([_np(rb[2 * b]["vt_o"]), _np(rb[2 * b + 1]["vt_o"])], axis=1)
        h = c % 2
        ims_c.append(dict(kt=np.ascontiguousarray(ktf[h * 512:(h + 1) * 512]),
                          vt=np.ascontiguousarray(vtf[h * 512:(h + 1) * 512]),
                          td=fm4(td[h * 512:(h + 1) * 512]),
                          tf=fm4(tfv[h * 512:(h + 1) * 512])))
    rc_ = _run(nc_c, ims_c).results
    nc_d = _prog("c_gate", build_attn_c, False, True)
    ims_d = []
    woT = np.ascontiguousarray(Wo.T)
    for c in range(NCORES):
        b = c // 2
        ytf = np.concatenate([_np(rc_[2 * b]["y_o"]), _np(rc_[2 * b + 1]["y_o"])], axis=0)
        ims_d.append(dict(x=x8[c], aot=np.ascontiguousarray(ytf[:, (c % 2) * 2048:(c % 2 + 1) * 2048]),
                          gt=_np(rb[c]["srt_o"]), wproj=woT))
    rd = _run(nc_d, ims_d).results
    return [np.ascontiguousarray(_np(rd[c]["xo"])) for c in range(NCORES)]


def kernel(**inputs):
    f32 = lambda k: np.ascontiguousarray(np.asarray(inputs[k], dtype=np.float32))
    x = f32("L0_attn_a_x")
    x8 = [np.ascontiguousarray(x[c // 2, (c % 2) * 2048:(c % 2 + 1) * 2048])
          for c in range(NCORES)]

    x8 = _sub_attn(x8, "s1", f32("L0_attn_a_w_qkv"), f32("L0_attn_a_w_o"))
    x8 = _sub_ffn(x8, f32("L0_ffn_W1"), f32("L0_ffn_b1"), f32("L0_ffn_W2"),
                  f32("L0_ffn_b2"), f32("L0_ffn_scale"))
    x8 = _sub_attn(x8, "s3", f32("L0_attn_b_w_qkv"), f32("L0_attn_b_w_proj"),
                   bias_qkv=f32("L0_attn_b_b_qkv"),
                   lora=(f32("L0_attn_b_a_q"), f32("L0_attn_b_b_q"),
                         f32("L0_attn_b_a_v"), f32("L0_attn_b_b_v")),
                   bproj=f32("L0_attn_b_b_proj"))
    x8 = _sub_ffn(x8, f32("L0_ffn_W1"), f32("L0_ffn_b1"), f32("L0_ffn_W2"),
                  f32("L0_ffn_b2"), f32("L0_ffn_scale"))
    x8 = _sub_attn(x8, "s5", f32("L1_attn_a_w_qkv"), f32("L1_attn_a_w_proj"))
    x8 = _sub_moe(x8, f32("L1_ffn_router_w"), f32("L1_ffn_w1_all"), f32("L1_ffn_w2_all"))
    x8 = _sub_wkv(x8, f32("L1_attn_b_time_decay"), f32("L1_attn_b_time_first"),
                  f32("L1_attn_b_time_mix_k").ravel(), f32("L1_attn_b_time_mix_v").ravel(),
                  f32("L1_attn_b_time_mix_r").ravel(),
                  f32("L1_attn_b_Wk"), f32("L1_attn_b_Wv"), f32("L1_attn_b_Wr"),
                  f32("L1_attn_b_Wo"))
    x8 = _sub_moe(x8, f32("L1_ffn_router_w"), f32("L1_ffn_w1_all"), f32("L1_ffn_w2_all"))

    out = np.zeros((NB, SEQ, D), np.float32)
    for c in range(NCORES):
        out[c // 2, (c % 2) * 2048:(c % 2 + 1) * 2048] = x8[c]
    return out
